# version 17
# baseline (speedup 1.0000x reference)
import numpy as np
import ml_dtypes
import concourse.bass as bass
import concourse.mybir as mybir
from concourse.ap import AP
from concourse.bass_utils import run_bass_kernel_spmd

AF = mybir.ActivationFunctionType
ALU = mybir.AluOpType
AX = mybir.AxisListType
F32 = mybir.dt.float32
BF16 = mybir.dt.bfloat16

N = 4096
D = 1024
H = 16
NT = 32
NI = 8
NJ = 32
ND = 8
NP = 8
PAIRS = [(0, 2), (4, 6), (8, 10), (12, 14), (1, 3), (5, 7), (9, 11), (13, 15)]

A_WSTB = 0
A_WT = 1024
A_QK2Q = [4096, 8192]
A_QK2K = [12288, 16384]
A_V2 = [20480, 24640]
A_E2 = 28800
A_CA = 30848
A_SUST = 31872
A_STG = 33920
A_END = 40064
A_XNB = 4096
A_CW = 0
A_SW = 8192
F_SUSB = 0
F_SUTM = 1024
F_OT2 = 2048
F_SUST = A_SUST // 2
F_STG = A_STG // 2


def mk_ap(t, part_off, col_off, dims):
    w = t.shape[1]
    ap = [[w, t.shape[0] - part_off]]
    for s, c in dims:
        ap.append([s, c])
    return AP(t, part_off * w + col_off, ap)


def build_nc(debug=False):
    nc = bass.Bass("TRN2", target_bir_lowering=False)
    x = nc.dram_tensor("x", [N, D], F32, kind="ExternalInput").ap()
    wq = nc.dram_tensor("wq", [3 * D, D], F32, kind="ExternalInput").ap()
    th = nc.dram_tensor("th", [D, D], F32, kind="ExternalInput").ap()
    thn = nc.dram_tensor("thn", [D, D], F32, kind="ExternalInput").ap()
    ph = nc.dram_tensor("ph", [D, D], F32, kind="ExternalInput").ap()
    phn = nc.dram_tensor("phn", [D, D], F32, kind="ExternalInput").ap()
    out = nc.dram_tensor("out", [N, D], F32, kind="ExternalOutput").ap()
    su_hbm = nc.dram_tensor("su_scratch", [H, N], F32, kind="Internal").ap()
    if debug:
        d_xnT = nc.dram_tensor("d_xnT", [128, 8 * N], F32, kind="ExternalOutput").ap()
        d_sT = nc.dram_tensor("d_sT", [128, 8 * N], F32, kind="ExternalOutput").ap()
        d_su = nc.dram_tensor("d_su", [H, N], F32, kind="ExternalOutput").ap()

    import contextlib
    with contextlib.ExitStack() as _stk:
        E = _stk.enter_context
        xnT = E(nc.sbuf_tensor("xnT", [128, 8 * N], BF16))
        sT = E(nc.sbuf_tensor("sT", [128, 8 * N], BF16))
        arena = E(nc.sbuf_tensor("arena", [128, A_END], BF16))
        identb = E(nc.sbuf_tensor("identb", [128, 128], BF16))
        identf = E(nc.sbuf_tensor("identf", [16, 16], F32))
        cva = E(nc.sbuf_tensor("cva", [128, 16], F32))
        cpi = E(nc.sbuf_tensor("cpi", [128, 3], F32))
        st2 = E(nc.psum_tensor("st2", [128, 2048], F32))
        oa = E(nc.psum_tensor("oa", [128, 512], F32))
        ob = E(nc.psum_tensor("ob", [128, 512], F32))
        qp = E(nc.psum_tensor("qp", [128, 512], F32))
        tp = E(nc.psum_tensor("tp", [128, 1024], BF16))
        names = ("sINIT sX sLN sLNd sLNs sLNa sTP sTPE sT2 sTCd sTCa sTC "
                 "sFM sSQ sMG sSQT sGS sWL sQP sQPE sQK sEXP sPV sEPI "
                 "sSUV sSUD sCA sSH sSUL sP3T sRC sRT sO3 sOD sX3").split()
        S_ = {nm: E(nc.semaphore(nm)) for nm in names}
        (sINIT, sX, sLN, sLNd, sLNs, sLNa, sTP, sTPE, sT2, sTCd, sTCa, sTC,
         sFM, sSQ, sMG, sSQT, sGS, sWL, sQP, sQPE, sQK, sEXP, sPV, sEPI,
         sSUV, sSUD, sCA, sSH, sSUL, sP3T, sRC, sRT, sO3, sOD, sX3) = (
            S_[k] for k in names)
        block = E(nc.Block())

        af = arena.bitcast(F32)

        def stgf(c0, c1, rows=slice(0, 128)):
            return af[rows, F_STG + c0: F_STG + c1]

        def su_st(k):
            o = F_SUST + (k % 2) * 512
            return af[64:65, o: o + 512]

        @block.gpsimd
        def _(gp):
            gp.memset(identb[:, :], 0.0)
            gp.drain()
            gp.affine_select(out=identb[:, :], in_=identb[:, :],
                             compare_op=ALU.not_equal, fill=1.0,
                             base=0, pattern=[[-1, 128]], channel_multiplier=1)
            gp.memset(identf[:, :], 0.0)
            gp.drain()
            gp.affine_select(out=identf[:, :], in_=identf[:, :],
                             compare_op=ALU.not_equal, fill=1.0,
                             base=0, pattern=[[-1, 16]], channel_multiplier=1)
            gp.memset(cpi[:, 0:1], float(np.pi / 2))
            gp.memset(cpi[:, 1:2], -1.0)
            gp.memset(cpi[:, 2:3], -1.0 / (1024.0 * 1024.0))
            gp.memset(arena[:, A_V2[0]: A_V2[0] + 4160], 1.0)
            gp.memset(arena[:, A_V2[1]: A_V2[1] + 4160], 1.0).then_inc(sINIT, 9)
            stF = sT.bitcast(F32)
            tdst = AP(stF, 0, [[16384, 128], [1024, 8], [1, 1024]])
            tdst2 = AP(stF, 8192, [[16384, 128], [1024, 8], [1, 1024]])
            gp.dma_start(out=tdst, in_=th.rearrange("(c p) o -> p c o", p=128)).then_inc(sT2, 16)
            gp.wait_ge(sT2, 16)
            gp.dma_start(out=tdst, in_=thn.rearrange("(c p) o -> p c o", p=128),
                         accum_op=ALU.add).then_inc(sT2, 16)
            gp.dma_start(out=tdst2, in_=ph.rearrange("(c p) o -> p c o", p=128)).then_inc(sT2, 16)
            gp.wait_ge(sT2, 48)
            gp.dma_start(out=tdst2, in_=phn.rearrange("(c p) o -> p c o", p=128),
                         accum_op=ALU.add).then_inc(sT2, 16)
            for q in range(NP):
                ha, hb = PAIRS[q]
                for c3, roff in enumerate((0, 64, 128)):
                    if q == 0 and c3 == 0:
                        gp.wait_ge(sFM, 64)
                    else:
                        gp.wait_ge(sTP, NT + 3 * q + c3)
                    for half, h in enumerate((ha, hb)):
                        r0 = h * 192 + roff
                        dst = arena[half * 64: half * 64 + 64, A_WSTB: A_WSTB + 1024]
                        gp.dma_start(out=dst, in_=wq[r0: r0 + 64, :]).then_inc(sWL, 16)
                if q > 0:
                    for k in range(16 * (q - 1), 16 * q):
                        Sg, hh = k // 2, k % 2
                        head = PAIRS[Sg // 8][hh]
                        is_ = Sg % 8
                        gp.wait_ge(sSUV, k + 1)
                        gp.dma_start(out=su_hbm[head, is_ * 512: is_ * 512 + 512],
                                     in_=su_st(k)).then_inc(sSUD, 16)
            for k in range(16 * 7, 16 * 8):
                Sg, hh = k // 2, k % 2
                head = PAIRS[Sg // 8][hh]
                is_ = Sg % 8
                gp.wait_ge(sSUV, k + 1)
                gp.dma_start(out=su_hbm[head, is_ * 512: is_ * 512 + 512],
                             in_=su_st(k)).then_inc(sSUD, 16)
            xnF = xnT.bitcast(F32)
            gp.wait_ge(sQP, NP * 24)
            for r in range(4):
                src = x.rearrange("(r4 p) d -> p r4 d", p=128)[:, 4 * r: 4 * r + 4, :]
                dst = AP(xnF, r * 4096, [[16384, 128], [1024, 4], [1, 1024]])
                gp.dma_start(out=dst, in_=src).then_inc(sX3, 16)
            for r in range(8, 16):
                gp.wait_ge(sGS, 8)
                if r > 8:
                    gp.wait_ge(sO3, 2 * r)
                src = x.rearrange("(r2 p) d -> p r2 d", p=128)[:, 2 * r: 2 * r + 2, :]
                dst = mk_ap(af, 0, F_STG, [(1024, 2), (1, 1024)])
                gp.dma_start(out=dst, in_=src).then_inc(sX3, 16)
            if debug:
                gp.wait_ge(sOD, 256)
                gp.dma_start(out=d_xnT[:, :], in_=xnT[:, :]).then_inc(sX3, 16)
                gp.dma_start(out=d_sT[:, :], in_=sT[:, :]).then_inc(sX3, 16)
                gp.dma_start(out=d_su[:, :], in_=su_hbm[:, :]).then_inc(sX3, 16)
                gp.wait_ge(sX3, 16 * 19)

        @block.sync
        def _(sy):
            for t in range(NT):
                if t >= 2:
                    sy.wait_ge(sLN, t - 1)
                sy.dma_start(out=stgf((t % 2) * D, (t % 2 + 1) * D),
                             in_=x[t * 128: (t + 1) * 128, :]).then_inc(sX, 16)
            for r in range(4):
                if r == 0:
                    sy.wait_ge(sSUD, 128 * 16)
                else:
                    sy.wait_ge(sP3T, r)
                sy.dma_start(out=af[0:16, F_SUSB: F_SUSB + 1024],
                             in_=su_hbm[:, r * 1024: (r + 1) * 1024]).then_inc(sSUL, 16)
            for r in range(8):
                sy.wait_ge(sO3, 4 * r + 4)
                dst = out.rearrange("(r4 p) d -> p r4 d", p=128)[:, 4 * r: 4 * r + 4, :]
                sy.dma_start(out=dst,
                             in_=mk_ap(af, 0, F_OT2, [(1024, 4), (1, 1024)])
                             ).then_inc(sOD, 16)
            sy.wait_ge(sOD, 16 * 8)

        def pe_setup_units(q):
            units = []
            for c3 in range(3):
                def u_w(te, q=q, c3=c3):
                    te.wait_ge(sWL, q * 96 + 32 * c3 + 32)
                    te.wait_ge(sTPE, NT + 3 * q + c3)
                    for d in range(ND):
                        mm = te.transpose(tp[:, d * 128:(d + 1) * 128],
                                          arena[:, A_WSTB + d * 128: A_WSTB + d * 128 + 128],
                                          identb[:, :])
                    mm.then_inc(sTP, 1)
                units.append(u_w)
            for c3 in range(2):
                for is_ in range(NI):
                    def u_qkv(te, q=q, c3=c3, is_=is_):
                        rnd = q * 24 + c3 * 8 + is_
                        te.wait_ge(sQPE, rnd)
                        if c3 == 0 and is_ == 0:
                            te.wait_ge(sTPE, NT + 3 * q + 3)
                        for d in range(ND):
                            mm = te.matmul(qp[:, :],
                                           arena[:, A_WT + d * 384 + c3 * 128: A_WT + d * 384 + c3 * 128 + 128],
                                           xnT[:, d * N + is_ * 512: d * N + is_ * 512 + 512],
                                           start=(d == 0), stop=(d == ND - 1))
                        mm.then_inc(sQP, 1)
                    units.append(u_qkv)
            for vb in range(8):
                def u_v(te, q=q, vb=vb):
                    rnd = q * 24 + 16 + vb
                    te.wait_ge(sQPE, rnd)
                    for tt in range(4):
                        j = 4 * vb + tt
                        for d in range(ND):
                            mm = te.matmul(qp[:, tt * 128:(tt + 1) * 128],
                                           xnT[:, d * N + j * 128: d * N + j * 128 + 128],
                                           arena[:, A_WT + d * 384 + 256: A_WT + d * 384 + 256 + 128],
                                           start=(d == 0), stop=(d == ND - 1))
                    mm.then_inc(sQP, 1)
                units.append(u_v)
            return units

        @block.tensor
        def _(te):
            for t in range(NT):
                te.wait_ge(sLN, t + 1)
                if t == 0:
                    te.wait_ge(sINIT, 9)
                else:
                    te.wait_ge(sTPE, t)
                xb = A_XNB + (t % 2) * 1024
                for c in range(ND):
                    mm = te.transpose(tp[:, c * 128:(c + 1) * 128],
                                      arena[:, xb + c * 128: xb + c * 128 + 128],
                                      identb[:, :])
                mm.then_inc(sTP, 1)
            for oc in range(8):
                for is_ in range(NI):
                    idx = oc * 8 + is_
                    if idx < 2:
                        te.wait_ge(sTC, 16)
                        te.wait_ge(sTPE, NT)
                    else:
                        te.wait_ge(sSQ, idx - 1)
                    half = (idx % 2) * 1024
                    for d in range(ND):
                        te.matmul(st2[:, half: half + 512],
                                  arena[:, A_CW + d * 1024 + oc * 128: A_CW + d * 1024 + oc * 128 + 128],
                                  xnT[:, d * N + is_ * 512: d * N + is_ * 512 + 512],
                                  start=(d == 0), stop=(d == ND - 1))
                    for d in range(ND):
                        mm = te.matmul(st2[:, half + 512: half + 1024],
                                       arena[:, A_SW + d * 1024 + oc * 128: A_SW + d * 1024 + oc * 128 + 128],
                                       xnT[:, d * N + is_ * 512: d * N + is_ * 512 + 512],
                                       start=(d == 0), stop=(d == ND - 1))
                    mm.then_inc(sFM, 1)
            for u in pe_setup_units(0):
                u(te)
            pend = None
            n_sh = 0
            shifts_due = []

            def emit_pv(te, pp, pis, pj):
                g = (pp * NI + pis) * NJ + pj
                h2 = (g % 2) * 1024
                te.wait_ge(sEXP, g + 1)
                if pj == 0:
                    te.wait_ge(sEPI, pp * NI + pis)
                    if pis == 0:
                        te.wait_ge(sQPE, pp * 24 + 24)
                pb = pp % 2
                te.matmul(oa[0:65, :],
                          arena[:, A_V2[pb] + pj * 65: A_V2[pb] + pj * 65 + 65],
                          arena[:, A_E2 + h2: A_E2 + h2 + 512],
                          start=(pj == 0), stop=(pj == NJ - 1),
                          skip_group_check=True)
                mm = te.matmul(ob[0:65, :],
                               arena[:, A_V2[pb] + 2080 + pj * 65: A_V2[pb] + 2080 + pj * 65 + 65],
                               arena[:, A_E2 + h2 + 512: A_E2 + h2 + 1024],
                               start=(pj == 0), stop=(pj == NJ - 1),
                               skip_group_check=True)
                if pj == NJ - 1:
                    mm.then_inc(sPV, 1)
                    if PAIRS[pp][0] % 2 == 1:
                        shifts_due.append((pp, pp * NI + pis))

            def emit_shifts(te):
                nonlocal n_sh
                while shifts_due:
                    shifts_due.pop(0)
                    for hh, oX in enumerate((oa, ob)):
                        te.wait_ge(sCA, 2 * n_sh + hh + 1)
                        te.matmul(oX[64:128, :], identb[0:64, 0:64],
                                  arena[0:64, A_CA + hh * 512: A_CA + hh * 512 + 512],
                                  start=True, stop=True, tile_position=(0, 64),
                                  skip_group_check=True).then_inc(sSH, 1)
                    n_sh += 1

            for p in range(NP):
                units = pe_setup_units(p + 1) if p + 1 < NP else []
                for is_ in range(NI):
                    Sg = p * NI + is_
                    for j in range(NJ):
                        g = Sg * NJ + j
                        h2 = (g % 2) * 1024
                        if g >= 2:
                            te.wait_ge(sEXP, g - 1)
                        else:
                            te.wait_ge(sSQ, 64)
                        if j == 0 and is_ == 0:
                            te.wait_ge(sQPE, p * 24 + 16)
                        te.matmul(st2[:, h2: h2 + 512],
                                  arena[0:64, A_QK2K[p % 2] + j * 128: A_QK2K[p % 2] + j * 128 + 128],
                                  arena[0:64, A_QK2Q[p % 2] + is_ * 512: A_QK2Q[p % 2] + is_ * 512 + 512],
                                  start=True, stop=True, tile_position=(0, 0),
                                  skip_group_check=True)
                        te.matmul(st2[:, h2 + 512: h2 + 1024],
                                  arena[64:128, A_QK2K[p % 2] + j * 128: A_QK2K[p % 2] + j * 128 + 128],
                                  arena[64:128, A_QK2Q[p % 2] + is_ * 512: A_QK2Q[p % 2] + is_ * 512 + 512],
                                  start=True, stop=True, tile_position=(64, 0),
                                  skip_group_check=True).then_inc(sQK, 1)
                        if pend is not None:
                            was_last = pend[2] == NJ - 1
                            emit_pv(te, *pend)
                            if was_last:
                                emit_shifts(te)
                        pend = (p, is_, j)
                    lo = is_ * len(units) // NI
                    hi = (is_ + 1) * len(units) // NI
                    for u in units[lo:hi]:
                        u(te)
            emit_pv(te, *pend)
            emit_shifts(te)
            for r in range(4):
                te.wait_ge(sSUL, (r + 1) * 16)
                if r == 0:
                    te.wait_ge(sQPE, NP * 24)
                for tt in range(8):
                    t = r * 8 + tt
                    mm = te.transpose(qp[:, t * 16: t * 16 + 16],
                                      af[0:16, F_SUSB + tt * 128: F_SUSB + tt * 128 + 128],
                                      identf[:, :])
                mm.then_inc(sP3T, 1)
            tpb3 = st2.bitcast(BF16)
            for t in range(NT):
                if t == 0:
                    te.wait_ge(sEPI, 64)
                    te.wait_ge(sEXP, 2048)
                elif t >= 2:
                    te.wait_ge(sO3, t - 1)
                o3 = (t % 2) * 1024
                for c in range(ND):
                    mm = te.transpose(tpb3[:, o3 + c * 128: o3 + (c + 1) * 128],
                                      sT[:, c * N + t * 128: c * N + t * 128 + 128],
                                      identb[:, :])
                mm.then_inc(sRT, 1)

        def dve_setup_units(q):
            units = []
            for c3 in range(3):
                def u_w(ve, q=q, c3=c3):
                    ve.wait_ge(sTP, NT + 3 * q + c3 + 1)
                    dst = mk_ap(arena, 0, A_WT + c3 * 128, [(384, 8), (1, 128)])
                    src = mk_ap(tp, 0, 0, [(128, 8), (1, 128)])
                    ve.tensor_copy(dst, src).then_inc(sTPE, 1)
                units.append(u_w)
            for c3 in range(2):
                for is_ in range(NI):
                    def u_qkv(ve, q=q, c3=c3, is_=is_):
                        base = (A_QK2Q, A_QK2K)[c3][q % 2]
                        rnd = q * 24 + c3 * 8 + is_
                        ve.wait_ge(sQP, rnd + 1)
                        ve.tensor_copy(arena[:, base + is_ * 512: base + is_ * 512 + 512],
                                       qp[:, :]).then_inc(sQPE, 1)
                    units.append(u_qkv)
            for vb in range(8):
                def u_v(ve, q=q, vb=vb):
                    rnd = q * 24 + 16 + vb
                    ve.wait_ge(sQP, rnd + 1)
                    for hh in range(2):
                        dst0 = A_V2[q % 2] + hh * 2080 + 4 * vb * 65
                        dst = mk_ap(arena, 0, dst0, [(65, 4), (1, 64)])
                        src = mk_ap(qp, 0, hh * 64, [(128, 4), (1, 64)])
                        cp = ve.tensor_copy(dst, src)
                    cp.then_inc(sQPE, 1)
                units.append(u_v)
            return units

        @block.vector
        def _(ve):
            def p0_stageA(t):
                sl = stgf((t % 2) * D, (t % 2 + 1) * D)
                o = (t % 2) * 8
                ve.wait_ge(sX, 16 * (t + 1))
                if t == 0:
                    ve.wait_ge(sINIT, 9)
                elif t >= 2:
                    ve.wait_ge(sLNa, t - 1)
                ve.reduce_sum(cva[:, o + 0: o + 1], sl, axis=AX.X)
                ve.drain()
                ve.tensor_tensor(cva[:, o + 1: o + 2], cva[:, o + 0: o + 1],
                                 cva[:, o + 0: o + 1], ALU.mult)
                ve.drain()
                ve.tensor_scalar(cva[:, o + 3: o + 4], cva[:, o + 1: o + 2],
                                 cpi[:, 2:3], 1e-5, ALU.mult, ALU.add).then_inc(sLNd, 1)

            def p0_stageB(t):
                sl = stgf((t % 2) * D, (t % 2 + 1) * D)
                o = (t % 2) * 8
                ve.wait_ge(sLNa, t + 1)
                ve.reciprocal(cva[:, o + 5: o + 6], cva[:, o + 4: o + 5])
                ve.drain()
                ve.tensor_scalar(cva[:, o + 6: o + 7], cva[:, o + 0: o + 1],
                                 cva[:, o + 5: o + 6], 1.0 / D, ALU.mult, ALU.mult)
                ve.drain()
                ve.wait_ge(sLNs, t + 1)
                xb = A_XNB + (t % 2) * 1024
                ve.tensor_scalar(arena[:, xb: xb + D], sl,
                                 cva[:, o + 5: o + 6], cva[:, o + 6: o + 7],
                                 ALU.mult, ALU.subtract).then_inc(sLN, 1)

            def p0_evac(t):
                ve.wait_ge(sTP, t + 1)
                dst = mk_ap(xnT, 0, t * 128, [(N, 8), (1, 128)])
                src = mk_ap(tp, 0, 0, [(128, 8), (1, 128)])
                ve.tensor_copy(dst, src).then_inc(sTPE, 1)

            for t in range(NT + 2):
                if t < NT:
                    p0_stageA(t)
                if 1 <= t <= NT:
                    p0_stageB(t - 1)
                if t >= 2:
                    p0_evac(t - 2)
            for c in range(8):
                ve.wait_ge(sTCa, c + 1)
                ve.tensor_scalar(arena[:, A_CW + c * 1024: A_CW + (c + 1) * 1024],
                                 stgf(2 * D, 3 * D), -1.0, None, ALU.add).then_inc(sTC, 1)
            for idx in range(64):
                ve.wait_ge(sSQ, idx + 1)
                if idx >= 2:
                    ve.wait_ge(sSQT, idx - 1)
                sq = stgf((idx % 2) * D, (idx % 2 + 1) * D)
                zb = stgf(2 * D + (idx % 2) * 512, 2 * D + (idx % 2) * 512 + 512)
                ve.tensor_add(zb, sq[:, 0:512], sq[:, 512:1024]).then_inc(sMG, 1)
            for u in dve_setup_units(0):
                u(ve)
            n_sh = 0
            for p in range(NP):
                ha, hb = PAIRS[p]
                odd = ha % 2 == 1
                units = dve_setup_units(p + 1) if p + 1 < NP else []
                for is_ in range(NI):
                    Sg = p * NI + is_
                    lo = is_ * len(units) // NI
                    hi = (is_ + 1) * len(units) // NI
                    for u in units[lo:hi]:
                        u(ve)
                    if is_ == 0 and p == 0:
                        ve.wait_ge(sGS, 8)
                    ve.wait_ge(sPV, Sg + 1)
                    for hh, (oX, head) in enumerate(((oa, ha), (ob, hb))):
                        k = 2 * Sg + hh
                        if k >= 2:
                            ve.wait_ge(sSUD, (k - 1) * 16)
                        ve.tensor_copy(su_st(k), oX[64:65, :]).then_inc(sSUV, 1)
                        if odd:
                            ve.tensor_copy(arena[0:64, A_CA + hh * 512: A_CA + hh * 512 + 512],
                                           oX[0:64, :]).then_inc(sCA, 1)
                    for hh, (oX, head) in enumerate(((oa, ha), (ob, hb))):
                        ch = head // 2
                        sl = sT[(head % 2) * 64:(head % 2) * 64 + 64,
                                ch * N + is_ * 512: ch * N + is_ * 512 + 512]
                        if odd:
                            ve.wait_ge(sSH, 2 * n_sh + hh + 1)
                            tt_ = ve.tensor_tensor(sl, oX[64:128, :], sl, ALU.mult)
                        else:
                            tt_ = ve.tensor_tensor(sl, oX[0:64, :], sl, ALU.mult)
                    tt_.then_inc(sEPI, 1)
                    if odd:
                        n_sh += 1
            tpb3v = st2.bitcast(BF16)
            ve.wait_ge(sP3T, 4)
            ve.reciprocal(af[:, F_SUTM: F_SUTM + 512], qp[:, :]).then_inc(sRC, 1)
            ve.drain()
            xnFv = xnT.bitcast(F32)
            for t in range(NT):
                ve.wait_ge(sRT, t + 1)
                if t < 16:
                    ve.wait_ge(sX3, ((t // 4) + 1) * 16)
                else:
                    ve.wait_ge(sX3, 64 + (t // 2 - 7) * 16)
                if t >= 4:
                    ve.wait_ge(sOD, (t // 4) * 16)
                o3 = (t % 2) * 1024
                if t < 16:
                    xsl = lambda h: xnFv[:, t * 1024 + h * 64: t * 1024 + h * 64 + 64]
                else:
                    xsl = lambda h: stgf((t % 2) * D + h * 64, (t % 2) * D + h * 64 + 64)
                for h in range(H):
                    st_ = ve.scalar_tensor_tensor(
                        af[:, F_OT2 + (t % 4) * D + h * 64: F_OT2 + (t % 4) * D + h * 64 + 64],
                        tpb3v[:, o3 + h * 64: o3 + h * 64 + 64],
                        af[:, F_SUTM + t * 16 + h: F_SUTM + t * 16 + h + 1],
                        xsl(h),
                        ALU.mult, ALU.add)
                st_.then_inc(sO3, 1)

        @block.scalar
        def _(sc):
            for t in range(NT):
                o = (t % 2) * 8
                sl = stgf((t % 2) * D, (t % 2 + 1) * D)
                sc.wait_ge(sX, 16 * (t + 1))
                if t >= 2:
                    sc.wait_ge(sTP, t - 1)
                xb = A_XNB + (t % 2) * 1024
                sc.activation(arena[:, xb: xb + D], sl, AF.Square,
                              accum_out=cva[:, o + 7: o + 8]).then_inc(sLNs, 1)
                sc.drain()
                sc.wait_ge(sLNd, t + 1)
                sc.activation(cva[:, o + 4: o + 5], cva[:, o + 7: o + 8], AF.Sqrt,
                              scale=1.0 / D, bias=cva[:, o + 3: o + 4]).then_inc(sLNa, 1)
            stFs = sT.bitcast(F32)
            for c in range(8):
                sc.wait_ge(sT2, 32)
                if c >= 1:
                    sc.wait_ge(sTC, c)
                sc.activation(stgf(2 * D, 3 * D), stFs[:, c * 1024: (c + 1) * 1024],
                              AF.Sin, bias=cpi[:, 0:1]).then_inc(sTCa, 1)
            for c in range(8):
                sc.wait_ge(sT2, 64)
                sc.activation(arena[:, A_SW + c * 1024: A_SW + (c + 1) * 1024],
                              stFs[:, 8192 + c * 1024: 8192 + (c + 1) * 1024],
                              AF.Sin).then_inc(sTC, 1)
            for idx in range(65):
                if idx < 64:
                    sc.wait_ge(sFM, idx + 1)
                    if idx >= 2:
                        sc.wait_ge(sMG, idx - 1)
                    sc.activation(stgf((idx % 2) * D, (idx % 2 + 1) * D),
                                  st2[:, (idx % 2) * 1024: (idx % 2) * 1024 + 1024],
                                  AF.Square).then_inc(sSQ, 1)
                if idx >= 1:
                    i2 = idx - 1
                    oc, is_ = i2 // 8, i2 % 8
                    sc.wait_ge(sMG, i2 + 1)
                    sc.activation(sT[:, oc * N + is_ * 512: oc * N + is_ * 512 + 512],
                                  stgf(2 * D + (i2 % 2) * 512, 2 * D + (i2 % 2) * 512 + 512),
                                  AF.Sqrt).then_inc(sSQT, 1)
            sc.drain()
            for oc in range(8):
                sc.activation(sT[:, oc * N: (oc + 1) * N],
                              sT[:, oc * N: (oc + 1) * N], AF.Silu).then_inc(sGS, 1)
            for g in range(2048):
                h2 = (g % 2) * 1024
                sc.wait_ge(sQK, g + 1)
                sc.activation(arena[:, A_E2 + h2: A_E2 + h2 + 1024],
                              st2[:, h2: h2 + 1024], AF.Exp,
                              scale=0.125).then_inc(sEXP, 1)

    return nc


def kernel(L0_attn_z, L0_attn_W_qkv, L0_ffn_theta, L0_ffn_phi,
           L0_ffn_theta_noise, L0_ffn_phi_noise):
    xs = np.ascontiguousarray(np.asarray(L0_attn_z, dtype=np.float32))
    wqa = np.ascontiguousarray(np.asarray(L0_attn_W_qkv, dtype=np.float32))
    tha = np.ascontiguousarray(np.asarray(L0_ffn_theta, dtype=np.float32))
    pha = np.ascontiguousarray(np.asarray(L0_ffn_phi, dtype=np.float32))
    thna = np.ascontiguousarray(np.asarray(L0_ffn_theta_noise, dtype=np.float32))
    phna = np.ascontiguousarray(np.asarray(L0_ffn_phi_noise, dtype=np.float32))
    nc = build_nc()
    in_maps = [
        {"x": xs[i], "wq": wqa, "th": tha, "thn": thna, "ph": pha, "phn": phna}
        for i in range(8)
    ]
    res = run_bass_kernel_spmd(nc, in_maps, core_ids=list(range(8)))
    return np.stack([np.asarray(res.results[i]["out"], dtype=np.float32)
                     for i in range(8)], axis=0)


if __name__ == "__main__":
    nc = build_nc()
    print("instructions:", nc.n_instructions())


# version 19
# speedup vs baseline: 1.1396x; 1.1396x over previous
import numpy as np
import ml_dtypes
import concourse.bass as bass
import concourse.mybir as mybir
from concourse.ap import AP
from concourse.bass_utils import run_bass_kernel_spmd

AF = mybir.ActivationFunctionType
ALU = mybir.AluOpType
AX = mybir.AxisListType
F32 = mybir.dt.float32
BF16 = mybir.dt.bfloat16

N = 4096
D = 1024
H = 16
NT = 32
NI = 8
NJ = 32
ND = 8
NP = 8
PAIRS = [(0, 2), (4, 6), (8, 10), (12, 14), (1, 3), (5, 7), (9, 11), (13, 15)]

A_WSTB = 0
A_WT = 1024
A_QK2Q = [4096, 8192]
A_QK2K = [12288, 16384]
A_V2 = [20480, 24640]
A_E2 = 28800
E2S = (28800, 29824, 33920)
A_CA = 30848
A_SUST = 31872
A_STG = 33920
A_END = 40064
A_XNB = 4096
A_CW = 0
A_SW = 8192
F_SUSB = 0
F_SUTM = 1024
F_OT2 = 2048
F_SUST = A_SUST // 2
F_STG = A_STG // 2


def mk_ap(t, part_off, col_off, dims):
    w = t.shape[1]
    ap = [[w, t.shape[0] - part_off]]
    for s, c in dims:
        ap.append([s, c])
    return AP(t, part_off * w + col_off, ap)


def build_nc(debug=False):
    nc = bass.Bass("TRN2", target_bir_lowering=False)
    x = nc.dram_tensor("x", [N, D], F32, kind="ExternalInput").ap()
    wq = nc.dram_tensor("wq", [3 * D, D], F32, kind="ExternalInput").ap()
    th = nc.dram_tensor("th", [D, D], F32, kind="ExternalInput").ap()
    thn = nc.dram_tensor("thn", [D, D], F32, kind="ExternalInput").ap()
    ph = nc.dram_tensor("ph", [D, D], F32, kind="ExternalInput").ap()
    phn = nc.dram_tensor("phn", [D, D], F32, kind="ExternalInput").ap()
    out = nc.dram_tensor("out", [N, D], F32, kind="ExternalOutput").ap()
    su_hbm = nc.dram_tensor("su_scratch", [H, N], F32, kind="Internal").ap()
    if debug:
        d_xnT = nc.dram_tensor("d_xnT", [128, 8 * N], F32, kind="ExternalOutput").ap()
        d_sT = nc.dram_tensor("d_sT", [128, 8 * N], F32, kind="ExternalOutput").ap()
        d_su = nc.dram_tensor("d_su", [H, N], F32, kind="ExternalOutput").ap()

    import contextlib
    with contextlib.ExitStack() as _stk:
        E = _stk.enter_context
        xnT = E(nc.sbuf_tensor("xnT", [128, 8 * N], BF16))
        sT = E(nc.sbuf_tensor("sT", [128, 8 * N], BF16))
        arena = E(nc.sbuf_tensor("arena", [128, A_END], BF16))
        identb = E(nc.sbuf_tensor("identb", [128, 128], BF16))
        identf = E(nc.sbuf_tensor("identf", [16, 16], F32))
        cva = E(nc.sbuf_tensor("cva", [128, 16], F32))
        cpi = E(nc.sbuf_tensor("cpi", [128, 3], F32))
        st2 = E(nc.psum_tensor("st2", [128, 2048], F32))
        oa = E(nc.psum_tensor("oa", [128, 512], F32))
        ob = E(nc.psum_tensor("ob", [128, 512], F32))
        qp = E(nc.psum_tensor("qp", [128, 512], F32))
        tp = E(nc.psum_tensor("tp", [128, 1024], BF16))
        names = ("sINIT sX sLN sLNd sLNs sLNa sTP sTPE sT2 sTCd sTCa sTC "
                 "sFM sSQ sMG sSQT sGS sWL sQP sQPE sQK sEXP sPV sEPI "
                 "sSUV sSUD sCA sSH sSUL sP3T sRC sRT sO3 sOD sX3 sVS sVT").split()
        S_ = {nm: E(nc.semaphore(nm)) for nm in names}
        (sINIT, sX, sLN, sLNd, sLNs, sLNa, sTP, sTPE, sT2, sTCd, sTCa, sTC,
         sFM, sSQ, sMG, sSQT, sGS, sWL, sQP, sQPE, sQK, sEXP, sPV, sEPI,
         sSUV, sSUD, sCA, sSH, sSUL, sP3T, sRC, sRT, sO3, sOD, sX3,
         sVS, sVT) = (S_[k] for k in names)
        block = E(nc.Block())

        af = arena.bitcast(F32)

        def stgf(c0, c1, rows=slice(0, 128)):
            return af[rows, F_STG + c0: F_STG + c1]

        def su_st(k):
            o = F_SUST + (k % 2) * 512
            return af[64:65, o: o + 512]

        @block.gpsimd
        def _(gp):
            gp.memset(identb[:, :], 0.0)
            gp.drain()
            gp.affine_select(out=identb[:, :], in_=identb[:, :],
                             compare_op=ALU.not_equal, fill=1.0,
                             base=0, pattern=[[-1, 128]], channel_multiplier=1)
            gp.memset(identf[:, :], 0.0)
            gp.drain()
            gp.affine_select(out=identf[:, :], in_=identf[:, :],
                             compare_op=ALU.not_equal, fill=1.0,
                             base=0, pattern=[[-1, 16]], channel_multiplier=1)
            gp.memset(cpi[:, 0:1], float(np.pi / 2))
            gp.memset(cpi[:, 1:2], -1.0)
            gp.memset(cpi[:, 2:3], -1.0 / (1024.0 * 1024.0))
            gp.memset(arena[:, A_V2[0]: A_V2[0] + 4160], 1.0)
            gp.memset(arena[:, A_V2[1]: A_V2[1] + 4160], 1.0).then_inc(sINIT, 9)
            stF = sT.bitcast(F32)
            tdst = AP(stF, 0, [[16384, 128], [1024, 8], [1, 1024]])
            tdst2 = AP(stF, 8192, [[16384, 128], [1024, 8], [1, 1024]])
            gp.dma_start(out=tdst, in_=th.rearrange("(c p) o -> p c o", p=128)).then_inc(sT2, 16)
            gp.wait_ge(sT2, 16)
            gp.dma_start(out=tdst, in_=thn.rearrange("(c p) o -> p c o", p=128),
                         accum_op=ALU.add).then_inc(sT2, 16)
            gp.dma_start(out=tdst2, in_=ph.rearrange("(c p) o -> p c o", p=128)).then_inc(sT2, 16)
            gp.wait_ge(sT2, 48)
            gp.dma_start(out=tdst2, in_=phn.rearrange("(c p) o -> p c o", p=128),
                         accum_op=ALU.add).then_inc(sT2, 16)
            for q in range(NP):
                ha, hb = PAIRS[q]
                for c3, roff in enumerate((0, 64, 128)):
                    if q == 0 and c3 == 0:
                        gp.wait_ge(sFM, 64)
                    else:
                        gp.wait_ge(sTP, NT + 3 * q + c3)
                    for half, h in enumerate((ha, hb)):
                        r0 = h * 192 + roff
                        dst = arena[half * 64: half * 64 + 64, A_WSTB: A_WSTB + 1024]
                        gp.dma_start(out=dst, in_=wq[r0: r0 + 64, :]).then_inc(sWL, 16)
                if q > 0:
                    for k in range(16 * (q - 1), 16 * q):
                        Sg, hh = k // 2, k % 2
                        head = PAIRS[Sg // 8][hh]
                        is_ = Sg % 8
                        gp.wait_ge(sSUV, k + 1)
                        gp.dma_start(out=su_hbm[head, is_ * 512: is_ * 512 + 512],
                                     in_=su_st(k)).then_inc(sSUD, 16)
            for k in range(16 * 7, 16 * 8):
                Sg, hh = k // 2, k % 2
                head = PAIRS[Sg // 8][hh]
                is_ = Sg % 8
                gp.wait_ge(sSUV, k + 1)
                gp.dma_start(out=su_hbm[head, is_ * 512: is_ * 512 + 512],
                             in_=su_st(k)).then_inc(sSUD, 16)
            xnF = xnT.bitcast(F32)
            gp.wait_ge(sQP, NP * 24)
            for r in range(4):
                src = x.rearrange("(r4 p) d -> p r4 d", p=128)[:, 4 * r: 4 * r + 4, :]
                dst = AP(xnF, r * 4096, [[16384, 128], [1024, 4], [1, 1024]])
                gp.dma_start(out=dst, in_=src).then_inc(sX3, 16)
            for r in range(8, 16):
                if r == 8:
                    gp.wait_ge(sEXP, 2048)
                else:
                    gp.wait_ge(sO3, 2 * r)
                src = x.rearrange("(r2 p) d -> p r2 d", p=128)[:, 2 * r: 2 * r + 2, :]
                dst = mk_ap(af, 0, F_STG, [(1024, 2), (1, 1024)])
                gp.dma_start(out=dst, in_=src).then_inc(sX3, 16)
            if debug:
                gp.wait_ge(sOD, 256)
                gp.dma_start(out=d_xnT[:, :], in_=xnT[:, :]).then_inc(sX3, 16)
                gp.dma_start(out=d_sT[:, :], in_=sT[:, :]).then_inc(sX3, 16)
                gp.dma_start(out=d_su[:, :], in_=su_hbm[:, :]).then_inc(sX3, 16)
                gp.wait_ge(sX3, 16 * 19)

        @block.sync
        def _(sy):
            for t in range(NT):
                if t >= 2:
                    sy.wait_ge(sLN, t - 1)
                sy.dma_start(out=stgf((t % 2) * D, (t % 2 + 1) * D),
                             in_=x[t * 128: (t + 1) * 128, :]).then_inc(sX, 16)
            for r in range(4):
                if r == 0:
                    sy.wait_ge(sSUD, 128 * 16)
                else:
                    sy.wait_ge(sP3T, r)
                sy.dma_start(out=af[0:16, F_SUSB: F_SUSB + 1024],
                             in_=su_hbm[:, r * 1024: (r + 1) * 1024]).then_inc(sSUL, 16)
            for r in range(8):
                sy.wait_ge(sO3, 4 * r + 4)
                dst = out.rearrange("(r4 p) d -> p r4 d", p=128)[:, 4 * r: 4 * r + 4, :]
                sy.dma_start(out=dst,
                             in_=mk_ap(af, 0, F_OT2, [(1024, 4), (1, 1024)])
                             ).then_inc(sOD, 16)
            sy.wait_ge(sOD, 16 * 8)

        def pe_setup_units(q):
            units = []
            for c3 in range(3):
                def u_w(te, q=q, c3=c3):
                    te.wait_ge(sWL, q * 96 + 32 * c3 + 32)
                    te.wait_ge(sTPE, NT + 3 * q + c3)
                    if c3 == 0:
                        te.wait_ge(sQPE, q * 24)
                    for d in range(ND):
                        mm = te.transpose(tp[:, d * 128:(d + 1) * 128],
                                          arena[:, A_WSTB + d * 128: A_WSTB + d * 128 + 128],
                                          identb[:, :])
                    mm.then_inc(sTP, 1)
                units.append(u_w)
            for c3 in range(2):
                for is_ in range(NI):
                    def u_qkv(te, q=q, c3=c3, is_=is_):
                        rnd = q * 24 + c3 * 8 + is_
                        te.wait_ge(sQPE, rnd)
                        if c3 == 0 and is_ == 0:
                            te.wait_ge(sTPE, NT + 3 * q + 3)
                        for d in range(ND):
                            mm = te.matmul(qp[:, :],
                                           arena[:, A_WT + d * 384 + c3 * 128: A_WT + d * 384 + c3 * 128 + 128],
                                           xnT[:, d * N + is_ * 512: d * N + is_ * 512 + 512],
                                           start=(d == 0), stop=(d == ND - 1))
                        mm.then_inc(sQP, 1)
                    units.append(u_qkv)
            va_units = []
            for vb in range(8):
                def u_va(te, q=q, vb=vb):
                    if vb == 0:
                        te.wait_ge(sQPE, q * 24 + 16)
                    else:
                        te.wait_ge(sVS, q * 8 + vb)
                    for d in range(ND):
                        mm = te.matmul(qp[:, :],
                                       arena[:, A_WT + d * 384 + 256: A_WT + d * 384 + 256 + 128],
                                       xnT[:, d * N + vb * 512: d * N + vb * 512 + 512],
                                       start=(d == 0), stop=(d == ND - 1))
                    mm.then_inc(sQP, 1)

                def u_vb(te, q=q, vb=vb):
                    te.wait_ge(sVS, q * 8 + vb + 1)
                    if vb >= 1:
                        te.wait_ge(sQPE, q * 24 + 16 + vb)
                    sb = A_STG + 2048 + (vb % 2) * 512
                    for tt in range(4):
                        mm = te.transpose(tp[:, tt * 128:(tt + 1) * 128],
                                          arena[:, sb + tt * 128: sb + tt * 128 + 128],
                                          identb[:, :])
                    mm.then_inc(sVT, 1)
                va_units.append((u_va, u_vb))
            units.append(va_units[0][0])
            for k in range(1, 8):
                units.append(va_units[k][0])
                units.append(va_units[k - 1][1])
            units.append(va_units[7][1])
            return units

        @block.tensor
        def _(te):
            for t in range(NT):
                te.wait_ge(sLN, t + 1)
                if t == 0:
                    te.wait_ge(sINIT, 9)
                else:
                    te.wait_ge(sTPE, t)
                xb = A_XNB + (t % 2) * 1024
                for c in range(ND):
                    mm = te.transpose(tp[:, c * 128:(c + 1) * 128],
                                      arena[:, xb + c * 128: xb + c * 128 + 128],
                                      identb[:, :])
                mm.then_inc(sTP, 1)
            for oc in range(8):
                for is_ in range(NI):
                    idx = oc * 8 + is_
                    if idx < 2:
                        te.wait_ge(sTC, 16)
                        te.wait_ge(sTPE, NT)
                    else:
                        te.wait_ge(sSQ, idx - 1)
                    half = (idx % 2) * 1024
                    for d in range(ND):
                        te.matmul(st2[:, half: half + 512],
                                  arena[:, A_CW + d * 1024 + oc * 128: A_CW + d * 1024 + oc * 128 + 128],
                                  xnT[:, d * N + is_ * 512: d * N + is_ * 512 + 512],
                                  start=(d == 0), stop=(d == ND - 1))
                    for d in range(ND):
                        mm = te.matmul(st2[:, half + 512: half + 1024],
                                       arena[:, A_SW + d * 1024 + oc * 128: A_SW + d * 1024 + oc * 128 + 128],
                                       xnT[:, d * N + is_ * 512: d * N + is_ * 512 + 512],
                                       start=(d == 0), stop=(d == ND - 1))
                    mm.then_inc(sFM, 1)
            for u in pe_setup_units(0):
                u(te)
            pendq = []
            n_sh = 0
            shifts_due = []

            def emit_pv(te, pp, pis, pj):
                g = (pp * NI + pis) * NJ + pj
                eb = E2S[g % 3]
                te.wait_ge(sEXP, g + 1)
                if pj == 0:
                    te.wait_ge(sEPI, pp * NI + pis)
                    if pis == 0:
                        te.wait_ge(sQPE, pp * 24 + 24)
                pb = pp % 2
                te.matmul(oa[0:65, :],
                          arena[:, A_V2[pb] + pj * 65: A_V2[pb] + pj * 65 + 65],
                          arena[:, eb: eb + 512],
                          start=(pj == 0), stop=(pj == NJ - 1),
                          skip_group_check=True)
                mm = te.matmul(ob[0:65, :],
                               arena[:, A_V2[pb] + 2080 + pj * 65: A_V2[pb] + 2080 + pj * 65 + 65],
                               arena[:, eb + 512: eb + 1024],
                               start=(pj == 0), stop=(pj == NJ - 1),
                               skip_group_check=True)
                if pj == NJ - 1:
                    mm.then_inc(sPV, 1)
                    if PAIRS[pp][0] % 2 == 1:
                        shifts_due.append((pp, pp * NI + pis))

            def emit_shifts(te):
                nonlocal n_sh
                while shifts_due:
                    shifts_due.pop(0)
                    for hh, oX in enumerate((oa, ob)):
                        te.wait_ge(sCA, 2 * n_sh + hh + 1)
                        te.matmul(oX[64:128, :], identb[0:64, 0:64],
                                  arena[0:64, A_CA + hh * 512: A_CA + hh * 512 + 512],
                                  start=True, stop=True, tile_position=(0, 64),
                                  skip_group_check=True).then_inc(sSH, 1)
                    n_sh += 1

            for p in range(NP):
                units = pe_setup_units(p + 1) if p + 1 < NP else []
                USLOT = (5, 11, 17, 23, 29)
                for is_ in range(NI):
                    Sg = p * NI + is_
                    lo = is_ * len(units) // NI
                    hi = (is_ + 1) * len(units) // NI
                    chunk = units[lo:hi]
                    for j in range(NJ):
                        g = Sg * NJ + j
                        h2 = (g % 2) * 1024
                        if g >= 2:
                            te.wait_ge(sEXP, g - 1)
                        else:
                            te.wait_ge(sSQ, 64)
                        if j == 0 and is_ == 0:
                            te.wait_ge(sQPE, p * 24 + 16)
                        te.matmul(st2[:, h2: h2 + 512],
                                  arena[0:64, A_QK2K[p % 2] + j * 128: A_QK2K[p % 2] + j * 128 + 128],
                                  arena[0:64, A_QK2Q[p % 2] + is_ * 512: A_QK2Q[p % 2] + is_ * 512 + 512],
                                  start=True, stop=True, tile_position=(0, 0),
                                  skip_group_check=True)
                        te.matmul(st2[:, h2 + 512: h2 + 1024],
                                  arena[64:128, A_QK2K[p % 2] + j * 128: A_QK2K[p % 2] + j * 128 + 128],
                                  arena[64:128, A_QK2Q[p % 2] + is_ * 512: A_QK2Q[p % 2] + is_ * 512 + 512],
                                  start=True, stop=True, tile_position=(64, 0),
                                  skip_group_check=True).then_inc(sQK, 1)
                        if len(pendq) == 2:
                            pq = pendq.pop(0)
                            was_last = pq[2] == NJ - 1
                            emit_pv(te, *pq)
                            if was_last:
                                emit_shifts(te)
                        pendq.append((p, is_, j))
                        if j in USLOT:
                            k = USLOT.index(j)
                            if k < len(chunk):
                                chunk[k](te)
            for pq in pendq:
                emit_pv(te, *pq)
                if pq[2] == NJ - 1:
                    emit_shifts(te)
            for r in range(4):
                te.wait_ge(sSUL, (r + 1) * 16)
                if r == 0:
                    te.wait_ge(sQPE, NP * 24)
                for tt in range(8):
                    t = r * 8 + tt
                    mm = te.transpose(qp[:, t * 16: t * 16 + 16],
                                      af[0:16, F_SUSB + tt * 128: F_SUSB + tt * 128 + 128],
                                      identf[:, :])
                mm.then_inc(sP3T, 1)
            tpb3 = st2.bitcast(BF16)
            for t in range(NT):
                if t == 0:
                    te.wait_ge(sEPI, 64)
                    te.wait_ge(sEXP, 2048)
                elif t >= 2:
                    te.wait_ge(sO3, t - 1)
                o3 = (t % 2) * 1024
                for c in range(ND):
                    mm = te.transpose(tpb3[:, o3 + c * 128: o3 + (c + 1) * 128],
                                      sT[:, c * N + t * 128: c * N + t * 128 + 128],
                                      identb[:, :])
                mm.then_inc(sRT, 1)

        def dve_setup_units(q):
            units = []
            for c3 in range(3):
                def u_w(ve, q=q, c3=c3):
                    ve.wait_ge(sTP, NT + 3 * q + c3 + 1)
                    dst = mk_ap(arena, 0, A_WT + c3 * 128, [(384, 8), (1, 128)])
                    src = mk_ap(tp, 0, 0, [(128, 8), (1, 128)])
                    ve.tensor_copy(dst, src).then_inc(sTPE, 1)
                units.append(u_w)
            for c3 in range(2):
                for is_ in range(NI):
                    def u_qkv(ve, q=q, c3=c3, is_=is_):
                        base = (A_QK2Q, A_QK2K)[c3][q % 2]
                        rnd = q * 24 + c3 * 8 + is_
                        ve.wait_ge(sQP, rnd + 1)
                        ve.tensor_copy(arena[:, base + is_ * 512: base + is_ * 512 + 512],
                                       qp[:, :]).then_inc(sQPE, 1)
                    units.append(u_qkv)
            va_units = []
            for vb in range(8):
                def u_stg(ve, q=q, vb=vb):
                    ve.wait_ge(sQP, q * 24 + 16 + vb + 1)
                    if q == 0 and vb == 0:
                        ve.wait_ge(sMG, 64)
                    if vb >= 2:
                        ve.wait_ge(sVT, q * 8 + vb - 1)
                    sb = A_STG + 2048 + (vb % 2) * 512
                    ve.tensor_copy(arena[:, sb: sb + 512],
                                   qp[:, :]).then_inc(sVS, 1)

                def u_evb(ve, q=q, vb=vb):
                    ve.wait_ge(sVT, q * 8 + vb + 1)
                    for hh in range(2):
                        dst0 = A_V2[q % 2] + hh * 2080 + 4 * vb * 65
                        dst = mk_ap(arena, 0, dst0, [(65, 4), (1, 64)])
                        src = mk_ap(tp, 0, hh * 64, [(128, 4), (1, 64)])
                        cp = ve.tensor_copy(dst, src)
                    cp.then_inc(sQPE, 1)
                va_units.append((u_stg, u_evb))
            units.append(va_units[0][0])
            for k in range(1, 8):
                units.append(va_units[k][0])
                units.append(va_units[k - 1][1])
            units.append(va_units[7][1])
            return units

        @block.vector
        def _(ve):
            def p0_stageA(t):
                sl = stgf((t % 2) * D, (t % 2 + 1) * D)
                o = (t % 2) * 8
                ve.wait_ge(sX, 16 * (t + 1))
                if t == 0:
                    ve.wait_ge(sINIT, 9)
                elif t >= 2:
                    ve.wait_ge(sLNa, t - 1)
                ve.reduce_sum(cva[:, o + 0: o + 1], sl, axis=AX.X)
                ve.drain()
                ve.tensor_tensor(cva[:, o + 1: o + 2], cva[:, o + 0: o + 1],
                                 cva[:, o + 0: o + 1], ALU.mult)
                ve.drain()
                ve.tensor_scalar(cva[:, o + 3: o + 4], cva[:, o + 1: o + 2],
                                 cpi[:, 2:3], 1e-5, ALU.mult, ALU.add).then_inc(sLNd, 1)

            def p0_stageB(t):
                sl = stgf((t % 2) * D, (t % 2 + 1) * D)
                o = (t % 2) * 8
                ve.wait_ge(sLNa, t + 1)
                ve.reciprocal(cva[:, o + 5: o + 6], cva[:, o + 4: o + 5])
                ve.drain()
                ve.tensor_scalar(cva[:, o + 6: o + 7], cva[:, o + 0: o + 1],
                                 cva[:, o + 5: o + 6], 1.0 / D, ALU.mult, ALU.mult)
                ve.drain()
                ve.wait_ge(sLNs, t + 1)
                xb = A_XNB + (t % 2) * 1024
                ve.tensor_scalar(arena[:, xb: xb + D], sl,
                                 cva[:, o + 5: o + 6], cva[:, o + 6: o + 7],
                                 ALU.mult, ALU.subtract).then_inc(sLN, 1)

            def p0_evac(t):
                ve.wait_ge(sTP, t + 1)
                dst = mk_ap(xnT, 0, t * 128, [(N, 8), (1, 128)])
                src = mk_ap(tp, 0, 0, [(128, 8), (1, 128)])
                ve.tensor_copy(dst, src).then_inc(sTPE, 1)

            for t in range(NT + 2):
                if t < NT:
                    p0_stageA(t)
                if 1 <= t <= NT:
                    p0_stageB(t - 1)
                if t >= 2:
                    p0_evac(t - 2)
            for c in range(8):
                ve.wait_ge(sTCa, c + 1)
                ve.tensor_scalar(arena[:, A_CW + c * 1024: A_CW + (c + 1) * 1024],
                                 stgf(2 * D, 3 * D), -1.0, None, ALU.add).then_inc(sTC, 1)
            for idx in range(64):
                ve.wait_ge(sSQ, idx + 1)
                if idx >= 2:
                    ve.wait_ge(sSQT, idx - 1)
                sq = stgf((idx % 2) * D, (idx % 2 + 1) * D)
                zb = stgf(2 * D + (idx % 2) * 512, 2 * D + (idx % 2) * 512 + 512)
                ve.tensor_add(zb, sq[:, 0:512], sq[:, 512:1024]).then_inc(sMG, 1)
            for u in dve_setup_units(0):
                u(ve)
            n_sh = 0
            for p in range(NP):
                ha, hb = PAIRS[p]
                odd = ha % 2 == 1
                units = dve_setup_units(p + 1) if p + 1 < NP else []
                for is_ in range(NI):
                    Sg = p * NI + is_
                    lo = is_ * len(units) // NI
                    hi = (is_ + 1) * len(units) // NI
                    for u in units[lo:hi]:
                        u(ve)
                    if is_ == 0 and p == 0:
                        ve.wait_ge(sGS, 8)
                    ve.wait_ge(sPV, Sg + 1)
                    for hh, (oX, head) in enumerate(((oa, ha), (ob, hb))):
                        k = 2 * Sg + hh
                        if k >= 2:
                            ve.wait_ge(sSUD, (k - 1) * 16)
                        ve.tensor_copy(su_st(k), oX[64:65, :]).then_inc(sSUV, 1)
                        if odd:
                            ve.tensor_copy(arena[0:64, A_CA + hh * 512: A_CA + hh * 512 + 512],
                                           oX[0:64, :]).then_inc(sCA, 1)
                    for hh, (oX, head) in enumerate(((oa, ha), (ob, hb))):
                        ch = head // 2
                        sl = sT[(head % 2) * 64:(head % 2) * 64 + 64,
                                ch * N + is_ * 512: ch * N + is_ * 512 + 512]
                        if odd:
                            ve.wait_ge(sSH, 2 * n_sh + hh + 1)
                            tt_ = ve.tensor_tensor(sl, oX[64:128, :], sl, ALU.mult)
                        else:
                            tt_ = ve.tensor_tensor(sl, oX[0:64, :], sl, ALU.mult)
                    tt_.then_inc(sEPI, 1)
                    if odd:
                        n_sh += 1
            tpb3v = st2.bitcast(BF16)
            ve.wait_ge(sP3T, 4)
            ve.reciprocal(af[:, F_SUTM: F_SUTM + 512], qp[:, :]).then_inc(sRC, 1)
            ve.drain()
            xnFv = xnT.bitcast(F32)
            for t in range(NT):
                ve.wait_ge(sRT, t + 1)
                if t < 16:
                    ve.wait_ge(sX3, ((t // 4) + 1) * 16)
                else:
                    ve.wait_ge(sX3, 64 + (t // 2 - 7) * 16)
                if t >= 4:
                    ve.wait_ge(sOD, (t // 4) * 16)
                o3 = (t % 2) * 1024
                if t < 16:
                    xsl = lambda h: xnFv[:, t * 1024 + h * 64: t * 1024 + h * 64 + 64]
                else:
                    xsl = lambda h: stgf((t % 2) * D + h * 64, (t % 2) * D + h * 64 + 64)
                for h in range(H):
                    st_ = ve.scalar_tensor_tensor(
                        af[:, F_OT2 + (t % 4) * D + h * 64: F_OT2 + (t % 4) * D + h * 64 + 64],
                        tpb3v[:, o3 + h * 64: o3 + h * 64 + 64],
                        af[:, F_SUTM + t * 16 + h: F_SUTM + t * 16 + h + 1],
                        xsl(h),
                        ALU.mult, ALU.add)
                st_.then_inc(sO3, 1)

        @block.scalar
        def _(sc):
            for t in range(NT):
                o = (t % 2) * 8
                sl = stgf((t % 2) * D, (t % 2 + 1) * D)
                sc.wait_ge(sX, 16 * (t + 1))
                if t >= 2:
                    sc.wait_ge(sTP, t - 1)
                xb = A_XNB + (t % 2) * 1024
                sc.activation(arena[:, xb: xb + D], sl, AF.Square,
                              accum_out=cva[:, o + 7: o + 8]).then_inc(sLNs, 1)
                sc.drain()
                sc.wait_ge(sLNd, t + 1)
                sc.activation(cva[:, o + 4: o + 5], cva[:, o + 7: o + 8], AF.Sqrt,
                              scale=1.0 / D, bias=cva[:, o + 3: o + 4]).then_inc(sLNa, 1)
            stFs = sT.bitcast(F32)
            for c in range(8):
                sc.wait_ge(sT2, 32)
                if c >= 1:
                    sc.wait_ge(sTC, c)
                sc.activation(stgf(2 * D, 3 * D), stFs[:, c * 1024: (c + 1) * 1024],
                              AF.Sin, bias=cpi[:, 0:1]).then_inc(sTCa, 1)
            for c in range(8):
                sc.wait_ge(sT2, 64)
                sc.activation(arena[:, A_SW + c * 1024: A_SW + (c + 1) * 1024],
                              stFs[:, 8192 + c * 1024: 8192 + (c + 1) * 1024],
                              AF.Sin).then_inc(sTC, 1)
            for idx in range(65):
                if idx < 64:
                    sc.wait_ge(sFM, idx + 1)
                    if idx >= 2:
                        sc.wait_ge(sMG, idx - 1)
                    sc.activation(stgf((idx % 2) * D, (idx % 2 + 1) * D),
                                  st2[:, (idx % 2) * 1024: (idx % 2) * 1024 + 1024],
                                  AF.Square).then_inc(sSQ, 1)
                if idx >= 1:
                    i2 = idx - 1
                    oc, is_ = i2 // 8, i2 % 8
                    sc.wait_ge(sMG, i2 + 1)
                    sc.activation(sT[:, oc * N + is_ * 512: oc * N + is_ * 512 + 512],
                                  stgf(2 * D + (i2 % 2) * 512, 2 * D + (i2 % 2) * 512 + 512),
                                  AF.Sqrt).then_inc(sSQT, 1)
            sc.drain()
            for oc in range(8):
                sc.activation(sT[:, oc * N: (oc + 1) * N],
                              sT[:, oc * N: (oc + 1) * N], AF.Silu).then_inc(sGS, 1)
            for g in range(2048):
                h2 = (g % 2) * 1024
                eb = E2S[g % 3]
                sc.wait_ge(sQK, g + 1)
                if g == 2:
                    sc.wait_ge(sMG, 64)
                sc.activation(arena[:, eb: eb + 1024],
                              st2[:, h2: h2 + 1024], AF.Exp,
                              scale=0.125).then_inc(sEXP, 1)

    return nc


def kernel(L0_attn_z, L0_attn_W_qkv, L0_ffn_theta, L0_ffn_phi,
           L0_ffn_theta_noise, L0_ffn_phi_noise):
    xs = np.ascontiguousarray(np.asarray(L0_attn_z, dtype=np.float32))
    wqa = np.ascontiguousarray(np.asarray(L0_attn_W_qkv, dtype=np.float32))
    tha = np.ascontiguousarray(np.asarray(L0_ffn_theta, dtype=np.float32))
    pha = np.ascontiguousarray(np.asarray(L0_ffn_phi, dtype=np.float32))
    thna = np.ascontiguousarray(np.asarray(L0_ffn_theta_noise, dtype=np.float32))
    phna = np.ascontiguousarray(np.asarray(L0_ffn_phi_noise, dtype=np.float32))
    nc = build_nc()
    in_maps = [
        {"x": xs[i], "wq": wqa, "th": tha, "thn": thna, "ph": pha, "phn": phna}
        for i in range(8)
    ]
    res = run_bass_kernel_spmd(nc, in_maps, core_ids=list(range(8)))
    return np.stack([np.asarray(res.results[i]["out"], dtype=np.float32)
                     for i in range(8)], axis=0)


if __name__ == "__main__":
    nc = build_nc()
    print("instructions:", nc.n_instructions())
